# version 04
# baseline (speedup 1.0000x reference)
import numpy as np
from contextlib import ExitStack
import concourse.bass as bass
import concourse.mybir as mybir
from concourse.bass_utils import run_bass_kernel_spmd

F32 = mybir.dt.float32
BF16 = mybir.dt.bfloat16
OP = mybir.AluOpType
AF = mybir.ActivationFunctionType

D = 2048
NT = 4096
TC = 512
NCH = NT // TC
NDT = D // 128
NI = 8192
NIT = NI // 128
NQ = NIT // 4
EPS = 1e-5


class Sem:
    def __init__(self, h):
        self.h = h
        self.n = 0


class Emit:
    KEYS = ("sync", "scalar", "vector", "tensor", "gpsimd")

    def __init__(self):
        self.ops = {k: [] for k in self.KEYS}

    def raw(self, ek, fn):
        self.ops[ek].append(fn)

    def wait(self, ek, sem, val=None):
        v = sem.n if val is None else val
        assert v <= sem.n, f"forward wait {ek} {v} > {sem.n}"
        if v > 0:
            self.raw(ek, lambda e, s=sem.h, vv=v: e.wait_ge(s, vv))

    def op(self, ek, build, inc=None, n=1):
        if inc is not None:
            inc.n += n

        def fn(e, b=build, i=inc, k=n):
            inst = b(e)
            if i is not None:
                inst.then_inc(i.h, k)

        self.raw(ek, fn)
        return inc.n if inc is not None else None


def build_nc(debug=False):
    nc = bass.Bass("TRN2", target_bir_lowering=False)

    xT = nc.dram_tensor("xT", [D, NT], F32, kind="ExternalInput").ap()
    W_inT = nc.dram_tensor("W_inT", [D, D], F32, kind="ExternalInput").ap()
    W_outT = nc.dram_tensor("W_outT", [D, D], F32, kind="ExternalInput").ap()
    wg = nc.dram_tensor("wg", [D, NI], F32, kind="ExternalInput").ap()
    wu = nc.dram_tensor("wu", [D, NI], F32, kind="ExternalInput").ap()
    wd = nc.dram_tensor("wd", [NI, D], F32, kind="ExternalInput").ap()
    dcols_in = nc.dram_tensor("dcols_in", [128, NDT], F32, kind="ExternalInput").ap()
    bcols_in = nc.dram_tensor("bcols_in", [128, NDT], F32, kind="ExternalInput").ap()
    xpre_in = nc.dram_tensor("xpre_in", [D, 256], F32, kind="ExternalInput").ap()
    onesrow_in = nc.dram_tensor("onesrow_in", [1, 128], F32, kind="ExternalInput").ap()
    onescol_in = nc.dram_tensor("onescol_in", [128, 1], F32, kind="ExternalInput").ap()
    outT = nc.dram_tensor("outT", [D, NT], F32, kind="ExternalOutput").ap()

    sk = "ExternalOutput" if debug else "Internal"
    stk_sp = nc.dram_tensor("stk_sp", [D, NT], BF16, kind=sk).ap()
    x2_sp = nc.dram_tensor("x2_sp", [D, NT], BF16, kind=sk).ap()
    xp_sp = nc.dram_tensor("xp_sp", [D, NT], F32, kind=sk).ap()
    hsp = nc.dram_tensor("hsp", [NIT, 128, NT], BF16, kind=sk).ap()

    xT_r = xT.rearrange("(dt p) s -> p dt s", p=128)
    W_inT_r = W_inT.rearrange("(dt p) e -> p dt e", p=128)
    W_outT_r = W_outT.rearrange("(et p) h -> p et h", p=128)
    wg_r = wg.rearrange("(dt p) i -> p dt i", p=128)
    wu_r = wu.rearrange("(dt p) i -> p dt i", p=128)
    wd_r = wd.rearrange("(it p) h -> p it h", p=128)
    stk_r = stk_sp.rearrange("(et p) s -> p et s", p=128)
    x2_r = x2_sp.rearrange("(dt p) s -> p dt s", p=128)
    xp_r = xp_sp.rearrange("(dt p) s -> p dt s", p=128)
    out_r = outT.rearrange("(dt p) s -> p dt s", p=128)
    hsp_r = hsp.rearrange("it p s -> p it s")

    es = ExitStack()
    def mks(name):
        return Sem(es.enter_context(nc.semaphore(name)))
    CONST = mks("CONST")
    XA = [mks("XA0"), mks("XA1")]
    STK = [mks("STK0"), mks("STK1")]
    XR = [mks("XR0"), mks("XR1")]
    X2H = mks("X2H")
    HIN = [mks("HIN0"), mks("HIN1")]
    XRC = [mks("XRC0"), mks("XRC1")]
    SPST = mks("SPST")
    SPXP = mks("SPXP")
    SPX2 = [mks("SPX20"), mks("SPX21")]
    SPH = [mks("SPH0"), mks("SPH1")]
    SPO = [mks("SPO0"), mks("SPO1")]
    LDG = mks("LDG")
    WQ = [mks("WQ0"), mks("WQ1")]
    WDB = [mks("WDB0"), mks("WDB1")]
    MM = mks("MM")
    VE = mks("VE")
    SE = mks("SE")
    ALLSEMS = [CONST, XA[0], XA[1], STK[0], STK[1], XR[0], XR[1], X2H,
               HIN[0], HIN[1], XRC[0], XRC[1], SPST, SPXP, SPX2[0], SPX2[1],
               SPH[0], SPH[1], SPO[0], SPO[1], LDG, WQ[0], WQ[1],
               WDB[0], WDB[1], MM, VE, SE]

    ps = [es.enter_context(nc.psum_tensor(f"ps{i}", [128, TC], F32)) for i in range(8)]
    S0, S1, B2, B3 = 0, 1, 2, 3

    def sb(name, shape, dt):
        return es.enter_context(nc.sbuf_tensor(name, shape, dt))

    dcols = sb("dcols", [128, NDT], F32)
    omd = sb("omd", [128, NDT], F32)
    biascols = sb("biascols", [128, NDT], F32)
    onesrow = sb("onesrow", [1, 128], F32)
    onesb = sb("onesb", [128, 1], BF16)
    state_cols = sb("state_cols", [128, NDT], F32)
    srows = sb("srows", [1, 3, TC], F32)
    bc_sb = sb("bc_sb", [128, 4, TC], F32)
    rot_sq = sb("rot_sq", [128, 4, TC], BF16)
    scan_data = sb("scan_data", [128, 2, TC], F32)
    scan_out = sb("scan_out", [128, TC], F32)
    tmp512 = sb("tmp512", [128, TC], F32)

    base0 = nc.sbuf_base

    def phase_alloc(tensors):
        nc.sbuf_base = base0
        out = {}
        for name, shape, dt in tensors:
            out[name] = nc.alloc_sbuf_tensor(name, shape, dt)
        assert nc.sbuf_base <= nc.sbuf_top, (
            f"SBUF overflow: base {nc.sbuf_base} top {nc.sbuf_top}")
        return out

    E = Emit()

    def gate_all(extra=()):
        for ek in E.KEYS:
            for s in ALLSEMS:
                E.wait(ek, s)

    E.op("sync", lambda e: e.dma_start(out=dcols[:], in_=dcols_in), CONST, 16)
    E.op("sync", lambda e: e.dma_start(out=biascols[:], in_=bcols_in), CONST, 16)
    E.op("sync", lambda e: e.dma_start(out=onesrow[:], in_=onesrow_in), CONST, 16)
    E.op("gpsimd", lambda e: e.dma_start(out=onesb[:], in_=onescol_in), LDG, 16)
    E.wait("scalar", CONST)
    E.op("scalar", lambda e: e.activation(dcols[:], dcols[:], AF.Sigmoid), SE)
    se_sig = SE.n
    E.wait("vector", SE, se_sig)
    E.op("vector", lambda e: e.tensor_scalar(
        out=omd[:], in0=dcols[:], scalar1=-1.0, scalar2=1.0,
        op0=OP.mult, op1=OP.add), VE)

    a1 = phase_alloc([
        ("W_inb", [128, NDT, D], BF16),
        ("xa", [128, 2, NDT, TC], F32),
        ("x1b", [128, 2, NDT, TC], BF16),
        ("stk_bf", [128, NDT, TC], BF16),
    ])
    W_inb, xa, x1b, stk_bf = a1["W_inb"], a1["xa"], a1["x1b"], a1["stk_bf"]

    E.op("gpsimd", lambda e: e.dma_start(out=W_inb[:], in_=W_inT_r), LDG, 16)
    E.op("gpsimd", lambda e: e.dma_start(out=onesb[:], in_=onescol_in), LDG, 16)

    mk = {}

    def load_xa(c):
        if c >= NCH:
            return
        if c == 1 and ("proj_end", -1) in mk:
            E.wait("sync", MM, mk[("proj_end", -1)])
            E.wait("sync", VE, mk[("ln_end", -1)])
        if c >= 2:
            E.wait("sync", VE, mk[("ln_end", c - 2)])
        E.op("sync", lambda e, c=c: e.dma_start(
            out=xa[:, c % 2], in_=xT_r[:, :, c * TC:(c + 1) * TC]), XA[c % 2], 16)

    def stats_block(c, src, kind, W=TC):
        p = c % 2
        if kind == "a1":
            E.wait("scalar", CONST if c == -1 else XA[c % 2])
        if kind == "a1" and c >= 2:
            E.wait("scalar", MM, mk[("proj_end", c - 2)])
        if kind == "a1" and c == 1 and ("proj_end", -1) in mk:
            E.wait("scalar", MM, mk[("proj_end", -1)])
        if kind == "a1" and c == 0 and ("stats_end", -1, "a1") in mk:
            E.wait("scalar", MM, mk[("stats_end", -1, "a1")])
        for dt in range(NDT):
            if dt >= 4 and ("stats", c, dt - 4, kind) in mk:
                E.wait("scalar", MM, mk[("stats", c, dt - 4, kind)])
            if kind == "a1":
                E.op("scalar", lambda e, dt=dt, p=p, W=W: e.copy(
                    x1b[:, p, dt, 0:W], xa[:, p, dt, 0:W]), SE)
                sq_src = lambda dt, p=p, W=W: xa[:, p, dt, 0:W]
            else:
                E.wait("scalar", VE, mk[("resid", c, dt)])
                E.op("scalar", lambda e, dt=dt, c=c: e.copy(
                    x2bf[:, dt % 4], src[:, dt]), SE)
                sq_src = lambda dt, W=W: src[:, dt]
            E.op("scalar", lambda e, dt=dt, W=W, f=sq_src: e.activation(
                rot_sq[:, dt % 4, 0:W], f(dt), AF.Square), SE)
            mk[("prep", c, dt, kind)] = SE.n
            E.wait("tensor", SE, mk[("prep", c, dt, kind)])
            if dt == 0 and mk.get(("stats_evac_prev", kind)) is not None:
                E.wait("tensor", SE, mk[("stats_evac_prev", kind)])
            sum_rhs = (lambda e, dt=dt, p=p, W=W: x1b[:, p, dt, 0:W]) \
                if kind == "a1" else (lambda e, dt=dt: x2bf[:, dt % 4])
            if dt == 0:
                E.wait("tensor", LDG)
            E.op("tensor", lambda e, dt=dt, W=W, f=sum_rhs: e.matmul(
                ps[S0][0:1, 0:W], onesb[:], f(e), start=(dt == 0), stop=(dt == NDT - 1),
                skip_group_check=True), None)
            E.op("tensor", lambda e, dt=dt, W=W: e.matmul(
                ps[S1][0:1, 0:W], onesb[:], rot_sq[:, dt % 4, 0:W],
                start=(dt == 0), stop=(dt == NDT - 1), skip_group_check=True), MM)
            mk[("stats", c, dt, kind)] = MM.n
        mk[("stats_end", c, kind)] = MM.n

    def rowchain(c, kind, W=TC):
        p = c % 2
        E.wait("scalar", MM, mk[("stats_end", c, kind)])
        E.op("scalar", lambda e, W=W: e.copy(srows[0:1, 0, 0:W], ps[S0][0:1, 0:W]), SE)
        E.op("scalar", lambda e, W=W: e.copy(srows[0:1, 1, 0:W], ps[S1][0:1, 0:W]), SE)
        mk[("stats_evac_prev", kind)] = SE.n
        se_evac = SE.n
        E.wait("vector", SE, se_evac)
        E.op("vector", lambda e, W=W: e.tensor_scalar(
            out=srows[0:1, 0, 0:W], in0=srows[0:1, 0, 0:W], scalar1=1.0 / D,
            scalar2=None, op0=OP.mult), None)
        E.op("vector", lambda e, W=W: e.tensor_scalar(
            out=srows[0:1, 1, 0:W], in0=srows[0:1, 1, 0:W], scalar1=1.0 / D,
            scalar2=None, op0=OP.mult), None)
        E.op("vector", lambda e, W=W: e.tensor_tensor(
            srows[0:1, 2, 0:W], srows[0:1, 0, 0:W], srows[0:1, 0, 0:W], op=OP.mult), None)
        E.op("vector", lambda e, W=W: e.tensor_tensor(
            srows[0:1, 1, 0:W], srows[0:1, 1, 0:W], srows[0:1, 2, 0:W], op=OP.subtract), None)
        E.op("vector", lambda e, W=W: e.tensor_scalar(
            out=srows[0:1, 1, 0:W], in0=srows[0:1, 1, 0:W], scalar1=EPS, scalar2=None,
            op0=OP.add), VE)
        ve_row = VE.n
        E.wait("scalar", VE, ve_row)
        E.op("scalar", lambda e, W=W: e.activation(srows[0:1, 1, 0:W], srows[0:1, 1, 0:W], AF.Ln), None)
        E.op("scalar", lambda e, W=W: e.activation(
            srows[0:1, 1, 0:W], srows[0:1, 1, 0:W], AF.Exp, scale=-0.5), SE)
        se_rstd = SE.n
        E.wait("vector", SE, se_rstd)
        E.op("vector", lambda e, W=W: e.tensor_tensor(
            srows[0:1, 2, 0:W], srows[0:1, 0, 0:W], srows[0:1, 1, 0:W], op=OP.mult), VE)
        ve_mr = VE.n
        E.wait("tensor", VE, ve_mr)
        E.wait("tensor", CONST)
        if mk.get(("bc_evac_prev", kind)) is not None:
            E.wait("tensor", SE, mk[("bc_evac_prev", kind)])
        E.op("tensor", lambda e, W=W: e.matmul(
            ps[B2][:, 0:W], onesrow[:], srows[0:1, 1, 0:W], start=True, stop=True,
            skip_group_check=True), None)
        E.op("tensor", lambda e, W=W: e.matmul(
            ps[B3][:, 0:W], onesrow[:], srows[0:1, 2, 0:W], start=True, stop=True,
            skip_group_check=True), MM)
        mm_bc = MM.n
        E.wait("scalar", MM, mm_bc)
        E.op("scalar", lambda e, p=p, W=W: e.copy(bc_sb[:, 2 * p, 0:W], ps[B2][:, 0:W]), SE)
        E.op("scalar", lambda e, p=p, W=W: e.copy(bc_sb[:, 2 * p + 1, 0:W], ps[B3][:, 0:W]), SE)
        mk[("bc_evac_prev", kind)] = SE.n
        mk[("bc_evac", c, kind)] = SE.n

    def ln_apply_a1(c, W=TC):
        p = c % 2
        E.wait("vector", SE, mk[("bc_evac", c, "a1")])
        E.wait("vector", MM, mk[("stats_end", c, "a1")])
        for dt in range(NDT):
            E.op("vector", lambda e, dt=dt, p=p, W=W: e.tensor_tensor(
                tmp512[:, 0:W], xa[:, p, dt, 0:W], bc_sb[:, 2 * p, 0:W],
                op=OP.mult), None)
            last = dt == NDT - 1
            E.op("vector", lambda e, dt=dt, p=p, W=W: e.tensor_tensor(
                x1b[:, p, dt, 0:W], tmp512[:, 0:W], bc_sb[:, 2 * p + 1, 0:W],
                op=OP.subtract), VE if last else None)
        mk[("ln_end", c)] = VE.n

    def proj_scan(c, W=TC):
        p = c % 2
        pre = c == -1
        E.wait("tensor", VE, mk[("ln_end", c)])
        E.wait("tensor", LDG)
        if c >= 1:
            E.wait("vector", SPST)
        for et in range(NDT):
            bank = 4 + et % 4
            if et >= 4:
                E.wait("tensor", VE, mk[("pevac", c, et - 4)])
            elif c >= 1:
                E.wait("tensor", VE, mk[("pevac", c - 1, NDT - 4 + et)])
            for dt in range(NDT):
                E.op("tensor", lambda e, dt=dt, et=et, p=p, bank=bank, W=W: e.matmul(
                    ps[bank][:, 0:W], W_inb[:, dt, et * 128:(et + 1) * 128],
                    x1b[:, p, dt, 0:W], start=(dt == 0), stop=(dt == NDT - 1),
                    skip_group_check=True), MM if dt == NDT - 1 else None)
            mk[("proj", c, et)] = MM.n
            E.wait("vector", MM, mk[("proj", c, et)])
            E.op("vector", lambda e, et=et, bank=bank, W=W: e.tensor_scalar(
                out=scan_data[:, et % 2, 0:W], in0=ps[bank][:, 0:W],
                scalar1=omd[:, et:et + 1], scalar2=None, op0=OP.mult), VE)
            mk[("pevac", c, et)] = VE.n
            E.wait("vector", VE, mk[("pevac", c, et)])
            init = 0.0 if pre else state_cols[:, et:et + 1]
            E.op("vector", lambda e, et=et, init=init, W=W: e.tensor_tensor_scan(
                out=scan_out[:, 0:W], data0=dcols[:, et:et + 1].broadcast_to([128, W]),
                data1=scan_data[:, et % 2, 0:W], initial=init,
                op0=OP.mult, op1=OP.add), None)
            E.op("vector", lambda e, et=et, W=W: e.tensor_copy(
                stk_bf[:, et, 0:W], scan_out[:, 0:W]), VE)
            E.op("vector", lambda e, et=et, W=W: e.tensor_copy(
                state_cols[:, et:et + 1], scan_out[:, W - 1:W]), VE)
        mk[("proj_end", c)] = MM.n
        mk[("scan_end", c)] = VE.n

    E.op("sync", lambda e: e.dma_start(
        out=xa[:, 1, :, 0:256],
        in_=xpre_in.rearrange("(dt p) s -> p dt s", p=128)), CONST, 16)
    stats_block(-1, None, "a1", W=256)
    rowchain(-1, "a1", W=256)
    ln_apply_a1(-1, W=256)
    proj_scan(-1, W=256)

    load_xa(0)
    load_xa(1)
    stats_block(0, None, "a1")
    rowchain(0, "a1")
    ln_apply_a1(0)

    for c in range(NCH):
        if c + 1 < NCH:
            stats_block(c + 1, None, "a1")
            rowchain(c + 1, "a1")
        if c >= 1:
            E.wait("scalar", VE, mk[("scan_end", c - 1)])
            E.op("scalar", lambda e, c=c: e.dma_start(
                out=stk_r[:, :, (c - 1) * TC:c * TC], in_=stk_bf[:]), SPST, 16)
        proj_scan(c)
        load_xa(c + 2)
        if c + 1 < NCH:
            ln_apply_a1(c + 1)

    E.wait("scalar", VE, mk[("scan_end", NCH - 1)])
    E.op("scalar", lambda e: e.dma_start(
        out=stk_r[:, :, (NCH - 1) * TC:NCH * TC], in_=stk_bf[:]), SPST, 16)
    a1_end = {s: s.n for s in ALLSEMS}
    gate_all()

    a2 = phase_alloc([
        ("W_outb", [128, NDT, D], BF16),
        ("stk_in", [128, 2, NDT, TC], BF16),
        ("xp", [128, NDT, TC], F32),
        ("xr", [128, 2, 4, TC], F32),
        ("x2bf_t", [128, 4, TC], BF16),
        ("x2out", [128, 2, 4, TC], BF16),
    ])
    W_outb, stk_in, xp = a2["W_outb"], a2["stk_in"], a2["xp"]
    xr = a2["xr"]
    x2bf, x2out = a2["x2bf_t"], a2["x2out"]

    mk[("stats_evac_prev", "a2")] = None
    mk[("bc_evac_prev", "a2")] = None

    for et in range(NDT):
        E.op("gpsimd", lambda e, et=et: e.dma_start(
            out=W_outb[:, et], in_=W_outT_r[:, et]), LDG, 16)
    E.op("gpsimd", lambda e: e.dma_start(out=onesb[:], in_=onescol_in), LDG, 16)

    def load_stk(c):
        if c >= NCH:
            return
        if c >= 2:
            E.wait("sync", MM, mk[("att_end", c - 2)])
        E.wait("sync", SPST)
        E.op("sync", lambda e, c=c: e.dma_start(
            out=stk_in[:, c % 2], in_=stk_r[:, :, c * TC:(c + 1) * TC]),
            STK[c % 2], 16)

    def load_xr(c, g):
        if c >= NCH or g >= 4:
            return
        key = ("xr_used", c, g - 2) if g >= 2 else ("xr_used", c - 1, g + 2)
        if key in mk:
            E.wait("sync", VE, mk[key])
        E.op("sync", lambda e, c=c, g=g: e.dma_start(
            out=xr[:, g % 2], in_=xT_r[:, 4 * g:4 * g + 4, c * TC:(c + 1) * TC]),
            XR[g % 2], 16)

    load_stk(0)
    load_stk(1)
    load_xr(0, 0)
    load_xr(0, 1)

    for c in range(NCH):
        p = c % 2
        E.wait("tensor", STK[c % 2])
        for ht in range(NDT):
            bank = 4 + ht % 4
            if ht >= 4:
                E.wait("tensor", SE, mk[("attevac", c, ht - 4)])
            elif c >= 1:
                E.wait("tensor", SE, mk[("attevac", c - 1, NDT - 4 + ht)])
            if c == 0 and ht == 0:
                E.wait("tensor", LDG)
            for et in range(NDT):
                E.op("tensor", lambda e, et=et, ht=ht, p=p, bank=bank: e.matmul(
                    ps[bank][:], W_outb[:, et, ht * 128:(ht + 1) * 128],
                    stk_in[:, p, et], start=(et == 0),
                    stop=(et == NDT - 1), skip_group_check=True),
                    MM if (et == NDT - 1) else None)
            mk[("att", c, ht)] = MM.n
            E.wait("scalar", MM, mk[("att", c, ht)])
            if ht == 0:
                if c >= 1:
                    E.wait("scalar", VE, mk[("ln2_end", c - 1)])
                    E.wait("scalar", SPXP)
            E.op("scalar", lambda e, ht=ht, bank=bank: e.activation(
                xp[:, ht], ps[bank][:], AF.Identity,
                bias=biascols[:, ht:ht + 1], scale=1.0), SE)
            mk[("attevac", c, ht)] = SE.n
            E.wait("vector", SE, mk[("attevac", c, ht)])
            E.wait("vector", XR[(ht // 4) % 2])
            E.op("vector", lambda e, ht=ht: e.tensor_tensor(
                xp[:, ht], xp[:, ht], xr[:, (ht // 4) % 2, ht % 4], op=OP.add), VE)
            mk[("resid", c, ht)] = VE.n
            if ht % 4 == 3:
                mk[("xr_used", c, ht // 4)] = VE.n
                load_xr(c, ht // 4 + 2)
                if ht // 4 == 3:
                    load_xr(c + 1, 0)
                    load_xr(c + 1, 1)
        mk[("att_end", c)] = MM.n
        mk[("resid_end", c)] = VE.n
        E.wait("scalar", VE, mk[("resid_end", c)])
        E.op("scalar", lambda e, c=c: e.dma_start(
            out=xp_r[:, :, c * TC:(c + 1) * TC], in_=xp[:]), SPXP, 16)
        stats_block(c, xp, "a2")
        rowchain(c, "a2")
        E.wait("vector", SE, mk[("bc_evac", c, "a2")])
        E.wait("vector", MM, mk[("stats_end", c, "a2")])
        for g in range(4):
            E.wait("vector", SPX2[g % 2])
            for j in range(4):
                dt = 4 * g + j
                E.op("vector", lambda e, dt=dt, p=p: e.tensor_tensor(
                    tmp512[:], xp[:, dt], bc_sb[:, 2 * p], op=OP.mult), None)
                E.op("vector", lambda e, dt=dt, g=g, p=p: e.tensor_tensor(
                    x2out[:, g % 2, dt % 4], tmp512[:], bc_sb[:, 2 * p + 1],
                    op=OP.subtract), VE)
            mk[("ln2g", c, g)] = VE.n
            E.wait("scalar", VE, mk[("ln2g", c, g)])
            E.op("scalar", lambda e, c=c, g=g: e.dma_start(
                out=x2_r[:, 4 * g:4 * g + 4, c * TC:(c + 1) * TC],
                in_=x2out[:, g % 2]), SPX2[g % 2], 16)
        mk[("ln2_end", c)] = VE.n
        load_stk(c + 2)

    gate_all()
    b = phase_alloc([
        ("x2half", [128, NDT, 2048], BF16),
        ("wqg", [128, 2, NDT, 512], BF16),
        ("wqu", [128, 2, NDT, 512], BF16),
        ("h_all", [128, 4, TC], F32),
        ("ha", [128, 4, TC], F32),
        ("tree", [128, 4, TC], F32),
        ("hsg", [128, 2, TC], F32),
        ("hsp_out", [128, 2, 4, TC], BF16),
    ])
    x2half, wqg, wqu = b["x2half"], b["wqg"], b["wqu"]
    h_all, ha, tree, hsg, hsp_out = b["h_all"], b["ha"], b["tree"], b["hsg"], b["hsp_out"]

    def load_x2half(hf):
        if hf == 1:
            E.wait("sync", MM, mk[("bq_end", 0, NQ - 1)])
        E.wait("sync", SPX2[0])
        E.wait("sync", SPX2[1])
        E.op("sync", lambda e, hf=hf: e.dma_start(
            out=x2half[:], in_=x2_r[:, :, hf * 2048:(hf + 1) * 2048]), X2H, 16)

    def load_wq(gq):
        if gq >= 2 * NQ:
            return
        hf, q = gq // NQ, gq % NQ
        if gq >= 2:
            E.wait("gpsimd", MM, mk[("bq_end", (gq - 2) // NQ, (gq - 2) % NQ)])
        E.op("gpsimd", lambda e, q=q, gq=gq: e.dma_start(
            out=wqg[:, gq % 2], in_=wg_r[:, :, q * 512:(q + 1) * 512]), WQ[gq % 2], 16)
        E.op("gpsimd", lambda e, q=q, gq=gq: e.dma_start(
            out=wqu[:, gq % 2], in_=wu_r[:, :, q * 512:(q + 1) * 512]), WQ[gq % 2], 16)

    load_x2half(0)
    load_wq(0)
    load_wq(1)
    E.op("gpsimd", lambda e: e.dma_start(out=onesb[:], in_=onescol_in), WQ[0], 16)
    E.op("gpsimd", lambda e: e.dma_start(out=onesb[:], in_=onescol_in), WQ[1], 16)

    bfree = {0: (None, None), 1: (None, None), 2: (None, None), 3: (None, None)}
    hsg_free = {0: None, 1: None}

    for gq in range(2 * NQ):
        hf, q = gq // NQ, gq % NQ
        E.wait("tensor", WQ[gq % 2])
        E.wait("tensor", X2H)
        for tc_i in range(4):
            for r in range(4):
                gbank = 2 * (r % 2)
                ubank = gbank + 1
                for bank in (gbank, ubank):
                    s, v = bfree[bank]
                    if s is not None:
                        E.wait("tensor", s, v)
                for dt in range(NDT):
                    E.op("tensor", lambda e, dt=dt, r=r, tc_i=tc_i, hf=hf, gq=gq,
                         gbank=gbank: e.matmul(
                        ps[gbank][:], wqg[:, gq % 2, dt, r * 128:(r + 1) * 128],
                        x2half[:, dt, tc_i * TC:(tc_i + 1) * TC],
                        start=(dt == 0), stop=(dt == NDT - 1),
                        skip_group_check=True), MM if dt == NDT - 1 else None)
                mm_g = MM.n
                for dt in range(NDT):
                    E.op("tensor", lambda e, dt=dt, r=r, tc_i=tc_i, hf=hf, gq=gq,
                         ubank=ubank: e.matmul(
                        ps[ubank][:], wqu[:, gq % 2, dt, r * 128:(r + 1) * 128],
                        x2half[:, dt, tc_i * TC:(tc_i + 1) * TC],
                        start=(dt == 0), stop=(dt == NDT - 1),
                        skip_group_check=True), MM if dt == NDT - 1 else None)
                mm_u = MM.n
                E.wait("scalar", MM, mm_g)
                if hsg_free[r % 2] is not None:
                    E.wait("scalar", VE, hsg_free[r % 2])
                E.op("scalar", lambda e, r=r, gbank=gbank: e.activation(
                    hsg[:, r % 2], ps[gbank][:], AF.Silu), SE)
                se_silu = SE.n
                bfree[gbank] = (SE, se_silu)
                E.wait("vector", SE, se_silu)
                E.wait("vector", MM, mm_u)
                E.op("vector", lambda e, r=r, ubank=ubank: e.tensor_tensor(
                    h_all[:, r], hsg[:, r % 2], ps[ubank][:], op=OP.mult), VE)
                mk[("bh", gq, tc_i, r)] = VE.n
                hsg_free[r % 2] = VE.n
                bfree[ubank] = (VE, VE.n)
            E.wait("scalar", VE, mk[("bh", gq, tc_i, 3)])
            for r in range(4):
                E.op("scalar", lambda e, r=r: e.activation(
                    ha[:, r], h_all[:, r], AF.Abs), SE if r == 3 else None)
            se_abs = SE.n
            E.wait("vector", SE, se_abs)
            E.wait("vector", SPH[tc_i % 2])
            T = tree
            E.op("vector", lambda e: e.tensor_tensor(T[:, 0], ha[:, 0], ha[:, 1], op=OP.max), None)
            E.op("vector", lambda e: e.tensor_tensor(T[:, 1], ha[:, 0], ha[:, 1], op=OP.min), None)
            E.op("vector", lambda e: e.tensor_tensor(T[:, 2], ha[:, 2], ha[:, 3], op=OP.max), None)
            E.op("vector", lambda e: e.tensor_tensor(T[:, 3], ha[:, 2], ha[:, 3], op=OP.min), None)
            E.op("vector", lambda e: e.tensor_tensor(T[:, 0], T[:, 0], T[:, 2], op=OP.min), None)
            E.op("vector", lambda e: e.tensor_tensor(T[:, 1], T[:, 1], T[:, 3], op=OP.max), None)
            E.op("vector", lambda e: e.tensor_tensor(T[:, 2], T[:, 0], T[:, 1], op=OP.max), None)
            for r in range(4):
                E.op("vector", lambda e, r=r: e.tensor_tensor(
                    T[:, 3], ha[:, r], T[:, 2], op=OP.is_ge), None)
                E.op("vector", lambda e, r=r, tc_i=tc_i: e.tensor_tensor(
                    hsp_out[:, tc_i % 2, r], h_all[:, r], T[:, 3], op=OP.mult),
                    VE if r == 3 else None)
            mk[("bsp", gq, tc_i)] = VE.n
            E.wait("scalar", VE, mk[("bsp", gq, tc_i)])
            E.op("scalar", lambda e, q=q, hf=hf, tc_i=tc_i: e.dma_start(
                out=hsp_r[:, 4 * q:4 * q + 4,
                          hf * 2048 + tc_i * TC: hf * 2048 + (tc_i + 1) * TC],
                in_=hsp_out[:, tc_i % 2]), SPH[tc_i % 2], 16)
        mk[("bq_end", hf, q)] = MM.n
        load_wq(gq + 2)
        if hf == 0 and q == NQ - 1:
            load_x2half(1)

    gate_all()
    cph = phase_alloc([
        ("wdb", [128, 2, NIT, 512], BF16),
        ("h_in", [128, 2, 8, TC], BF16),
        ("xr2", [128, 2, 4, TC], F32),
        ("out_st", [128, 2, 4, TC], F32),
    ])
    wdb, h_in, xr2, out_st = cph["wdb"], cph["h_in"], cph["xr2"], cph["out_st"]

    def load_wdb(ph):
        if ph >= 4:
            return
        if ph >= 2:
            E.wait("gpsimd", MM, mk[("c_end", ph - 2)])
        E.op("gpsimd", lambda e, ph=ph: e.dma_start(
            out=wdb[:, ph % 2], in_=wd_r[:, :, ph * 512:(ph + 1) * 512]), WDB[ph % 2], 16)
        E.op("gpsimd", lambda e: e.dma_start(out=onesb[:], in_=onescol_in),
             WDB[ph % 2], 16)

    def load_hin(ph, tc_i, g):
        key = ("hing", ph, tc_i, g - 2)
        key2 = ("hing", ph, tc_i - 1, g + 6)
        key3 = ("hing", ph - 1, 7, g + 6)
        for k in (key, key2, key3):
            if k in mk:
                E.wait("sync", MM, mk[k])
                break
        E.op("sync", lambda e, ph=ph, tc_i=tc_i, g=g: e.dma_start(
            out=h_in[:, g % 2], in_=hsp_r[:, g * 8:(g + 1) * 8,
                                          tc_i * TC:(tc_i + 1) * TC]),
            HIN[g % 2], 16)
        mk[("hin", ph, tc_i, g)] = (HIN[g % 2], HIN[g % 2].n)

    def load_xr2(ph, tc_i):
        if ("outevac", ph, tc_i - 2) in mk:
            E.wait("sync", VE, mk[("outevac", ph, tc_i - 2)])
        elif ph > 0 and ("outevac", ph - 1, tc_i + 6) in mk:
            E.wait("sync", VE, mk[("outevac", ph - 1, tc_i + 6)])
        E.wait("sync", SPXP)
        E.op("sync", lambda e, ph=ph, tc_i=tc_i: e.dma_start(
            out=xr2[:, tc_i % 2], in_=xp_r[:, 4 * ph:4 * ph + 4,
                                           tc_i * TC:(tc_i + 1) * TC]),
            XRC[tc_i % 2], 16)

    load_wdb(0)
    load_wdb(1)

    for ph in range(4):
        for tc_i in range(8):
            if tc_i == 0:
                load_hin(ph, 0, 0)
                load_hin(ph, 0, 1)
                load_xr2(ph, 0)
            E.wait("tensor", WDB[ph % 2])
            if ("outevac", ph, tc_i - 2) in mk:
                E.wait("tensor", VE, mk[("outevac", ph, tc_i - 2)])
            elif ph > 0 and ("outevac", ph - 1, tc_i + 6) in mk:
                E.wait("tensor", VE, mk[("outevac", ph - 1, tc_i + 6)])
            for it in range(NIT):
                g = it // 8
                if it % 8 == 0:
                    s_, v_ = mk[("hin", ph, tc_i, g)]
                    E.wait("tensor", s_, v_)
                for ht4 in range(4):
                    bank = (tc_i % 2) * 4 + ht4
                    E.op("tensor", lambda e, it=it, ht4=ht4, ph=ph, bank=bank, g=g:
                         e.matmul(
                        ps[bank][:], wdb[:, ph % 2, it, ht4 * 128:(ht4 + 1) * 128],
                        h_in[:, g % 2, it % 8], start=(it == 0), stop=(it == NIT - 1),
                        skip_group_check=True),
                        MM if (ht4 == 3) else None)
                if it % 8 == 7:
                    mk[("hing", ph, tc_i, g)] = MM.n
                    nxt = (ph, tc_i, g + 2)
                    if g + 2 < 8:
                        load_hin(ph, tc_i, g + 2)
                    elif tc_i + 1 < 8:
                        if g + 2 == 8:
                            load_hin(ph, tc_i + 1, 0)
                            load_xr2(ph, tc_i + 1)
                        elif g + 2 == 9:
                            load_hin(ph, tc_i + 1, 1)
                    elif ph + 1 < 4:
                        if g + 2 == 8:
                            load_hin(ph + 1, 0, 0)
                            load_xr2(ph + 1, 0)
                        elif g + 2 == 9:
                            load_hin(ph + 1, 0, 1)
            mm_c_tc = MM.n
            E.wait("vector", MM, mm_c_tc)
            E.wait("vector", XRC[tc_i % 2])
            E.wait("vector", SPO[tc_i % 2])
            for ht4 in range(4):
                bank = (tc_i % 2) * 4 + ht4
                E.op("vector", lambda e, ht4=ht4, tc_i=tc_i, bank=bank: e.tensor_tensor(
                    out_st[:, tc_i % 2, ht4], ps[bank][:], xr2[:, tc_i % 2, ht4],
                    op=OP.add), VE if ht4 == 3 else None)
            mk[("outevac", ph, tc_i)] = VE.n
            E.wait("scalar", VE, mk[("outevac", ph, tc_i)])
            E.op("scalar", lambda e, ph=ph, tc_i=tc_i: e.dma_start(
                out=out_r[:, 4 * ph:4 * ph + 4, tc_i * TC:(tc_i + 1) * TC],
                in_=out_st[:, tc_i % 2]), SPO[tc_i % 2], 16)
        mk[("c_end", ph)] = MM.n
        load_wdb(ph + 2)

    gate_all()

    with nc.Block() as block:
        @block.sync
        def _(eng):
            for f in E.ops["sync"]:
                f(eng)

        @block.scalar
        def _(eng):
            for f in E.ops["scalar"]:
                f(eng)

        @block.vector
        def _(eng):
            for f in E.ops["vector"]:
                f(eng)

        @block.tensor
        def _(eng):
            for f in E.ops["tensor"]:
                f(eng)

        @block.gpsimd
        def _(eng):
            for f in E.ops["gpsimd"]:
                f(eng)

    nc._keepalive = es
    return nc


def _perm():
    q, p, r = np.meshgrid(np.arange(16), np.arange(128), np.arange(4),
                          indexing="ij")
    perm = np.empty(8192, np.int64)
    perm[(128 * (4 * q.ravel() + r.ravel()) + p.ravel())] = \
        512 * q.ravel() + 4 * p.ravel() + r.ravel()
    return perm


def kernel(L0_attn_x, L0_attn_W_in, L0_attn_W_out, L0_attn_decay, L0_attn_bias,
           L0_ffn_w_gate, L0_ffn_w_up, L0_ffn_w_down):
    x = np.asarray(L0_attn_x)
    B, S, _ = x.shape
    half = S // 2
    perm = _perm()

    W_inT = np.ascontiguousarray(np.asarray(L0_attn_W_in).T)
    W_outT = np.ascontiguousarray(np.asarray(L0_attn_W_out).T)
    wg = np.ascontiguousarray(np.asarray(L0_ffn_w_gate)[:, perm])
    wu = np.ascontiguousarray(np.asarray(L0_ffn_w_up)[:, perm])
    wd = np.ascontiguousarray(np.asarray(L0_ffn_w_down)[perm, :])
    dcols = np.ascontiguousarray(np.asarray(L0_attn_decay).reshape(NDT, 128).T)
    bcols = np.ascontiguousarray(np.asarray(L0_attn_bias).reshape(NDT, 128).T)
    onesrow = np.ones((1, 128), np.float32)
    onescol = np.ones((128, 1), np.float32)

    in_maps = []
    for k in range(8):
        b_i, h_i = k // 2, k % 2
        xTk = np.ascontiguousarray(x[b_i, h_i * half:(h_i + 1) * half, :].T)
        if h_i == 1:
            xpre = np.ascontiguousarray(x[b_i, half - 256:half, :].T)
        else:
            xpre = np.zeros((D, 256), np.float32)
        in_maps.append(dict(
            xT=xTk, W_inT=W_inT, W_outT=W_outT, wg=wg, wu=wu, wd=wd,
            dcols_in=dcols, bcols_in=bcols, xpre_in=xpre,
            onesrow_in=onesrow, onescol_in=onescol))

    nc = build_nc()
    res = run_bass_kernel_spmd(nc, in_maps, core_ids=list(range(8)))

    out = np.empty((B, S, D), np.float32)
    for k in range(8):
        b_i, h_i = k // 2, k % 2
        out[b_i, h_i * half:(h_i + 1) * half, :] = res.results[k]["outT"].T
    return out
